# version 01 (final)
# baseline (speedup 1.0000x reference)
import numpy as np
import ml_dtypes
import concourse.bass as bass
import concourse.mybir as mybir
from concourse.bass_utils import run_bass_kernel_spmd

F32 = mybir.dt.float32
BF16 = mybir.dt.bfloat16
AF = mybir.ActivationFunctionType
ALU = mybir.AluOpType
AX = mybir.AxisListType
BF = ml_dtypes.bfloat16

T = 2048
D = 2048
NCH = D // 128
NEG = -3.0e38


ENG = ("t", "v", "s", "g", "y")
DMA_TOUCH = False
PAD_BYTES = 0
PREFETCH = True
MM_DRAIN_GUARD = False
NDMA = 8
DMA_ROT = {"y": 4, "g": 4, "s": 3}
PAGE = 4096


class Op:
    __slots__ = ("engine", "stream", "fn", "reads", "writes", "seq",
                 "edges", "waits", "inc", "inc_val", "tag", "pre_drain")

    def __init__(self):
        self.edges = {}
        self.waits = []
        self.inc = False
        self.inc_val = 0
        self.pre_drain = False


class Rec:
    def __init__(self):
        self.ops = []
        self.key_state = {}
        self.pages = {}
        self.dma_counts = [0] * NDMA
        self.n_dma = 0
        self.qrot = {}
        self.last_on_sem = {}

    def _overlapping(self, space, start, end):
        out = set()
        for pg in range(start // PAGE, (end - 1) // PAGE + 1):
            for k in self.pages.get((space, pg), ()):
                if k[1] < end and start < k[2]:
                    out.add(k)
        return out

    def _add_key(self, key):
        if key not in self.key_state:
            self.key_state[key] = {"writer": None, "readers": {}}
            for pg in range(key[1] // PAGE, (key[2] - 1) // PAGE + 1):
                self.pages.setdefault((key[0], pg), set()).add(key)
        return self.key_state[key]

    def _del_key(self, key):
        self.key_state.pop(key, None)
        for pg in range(key[1] // PAGE, (key[2] - 1) // PAGE + 1):
            s = self.pages.get((key[0], pg))
            if s is not None:
                s.discard(key)

    def add(self, engine, fn, reads=(), writes=(), tag="", pre_drain=False):
        op = Op()
        op.seq = len(self.ops)
        op.tag = tag
        op.fn = fn
        op.pre_drain = pre_drain
        op.reads = list(reads)
        op.writes = list(writes)
        if isinstance(engine, tuple):
            op.engine = engine[1]
            q = engine[1]
            nrot = DMA_ROT.get(q, 2)
            idx = self.qrot.get(q, 0)
            self.qrot[q] = (idx + 1) % nrot
            op.stream = "d_%s%d" % (q, idx)
            op.inc = True
            op.inc_val = 16
            prev = self.last_on_sem.get(op.stream)
            if prev is not None:
                op.edges[op.stream] = prev
            self.last_on_sem[op.stream] = op.seq
        else:
            op.engine = engine
            op.stream = engine
        self.ops.append(op)

        def edge(stream, seq):
            if seq == op.seq:
                return
            prev = op.edges.get(stream, -1)
            if seq > prev:
                op.edges[stream] = seq

        for (sp, a, b) in op.reads:
            keys = self._overlapping(sp, a, b)
            st = None
            for k in keys:
                ks = self.key_state[k]
                if ks["writer"] is not None:
                    edge(*ks["writer"])
            key = (sp, a, b)
            st = self._add_key(key)
            prev = st["readers"].get(op.stream, -1)
            if op.seq > prev:
                st["readers"][op.stream] = op.seq
        for (sp, a, b) in op.writes:
            keys = self._overlapping(sp, a, b)
            key = (sp, a, b)
            for k in keys:
                ks = self.key_state[k]
                if ks["writer"] is not None:
                    edge(*ks["writer"])
                for stn, sq in ks["readers"].items():
                    edge(stn, sq)
                if k != key:
                    frag_state = {"writer": ks["writer"],
                                  "readers": dict(ks["readers"])}
                    self._del_key(k)
                    if k[1] < a:
                        fs = self._add_key((sp, k[1], a))
                        fs.update({"writer": frag_state["writer"],
                                   "readers": dict(frag_state["readers"])})
                    if k[2] > b:
                        fs = self._add_key((sp, b, k[2]))
                        fs.update({"writer": frag_state["writer"],
                                   "readers": dict(frag_state["readers"])})
            st = self._add_key(key)
            st["writer"] = (op.stream, op.seq)
            st["readers"] = {}
        return op

    def dma(self, issue_engine, out_ap, in_ap, reads, writes, tag=""):
        def fn(eng, o=out_ap, i=in_ap):
            return eng.dma_start(out=o, in_=i)
        return self.add(("dma", issue_engine), fn, reads, writes, tag=tag)

    def finalize(self, nc, serial=False, serial_eng=None, strict=False):
        if not strict:
            for op in self.ops:
                if op.stream in ENG and op.stream in op.edges:
                    del op.edges[op.stream]
        else:
            for op in self.ops:
                if op.stream in ENG and op.edges.get(op.stream) == op.seq:
                    del op.edges[op.stream]
        if serial:
            for i, op in enumerate(self.ops):
                if i > 0:
                    op.edges = {self.ops[i - 1].stream: i - 1}
                else:
                    op.edges = {}
        elif serial_eng:
            last = {}
            for i, op in enumerate(self.ops):
                if op.engine in serial_eng or (i > 0 and self.ops[i - 1].engine in serial_eng):
                    for stn, sq in last.items():
                        if stn != op.stream:
                            op.edges[stn] = sq
                last[op.stream] = i
        need_inc = set()
        for op in self.ops:
            for stream, seq in op.edges.items():
                if stream in ENG:
                    need_inc.add(seq)
        if serial or serial_eng:
            for op in self.ops:
                if op.stream in ENG:
                    need_inc.add(op.seq)
        inc_count = {}
        op_value = {}
        for op in self.ops:
            if op.stream in ENG:
                if op.seq in need_inc:
                    op.inc = True
                    op.inc_val = 1
            if op.inc:
                c = inc_count.get(op.stream, 0) + op.inc_val
                inc_count[op.stream] = c
                op_value[op.seq] = c
        high = {}
        for op in self.ops:
            for stream, seq in sorted(op.edges.items()):
                if seq not in op_value:
                    continue
                val = op_value[seq]
                hk = (op.engine, stream)
                if high.get(hk, 0) >= val:
                    continue
                high[hk] = val
                op.waits.append((stream, val))
        self.sems = {}
        for e in ENG:
            self.sems[e] = nc.alloc_semaphore("sem_" + e)
        for q, n in DMA_ROT.items():
            for i in range(n):
                self.sems["d_%s%d" % (q, i)] = nc.alloc_semaphore("sem_d_%s%d" % (q, i))
        self.final_counts = dict(inc_count)

    def verify(self):
        per_stream = {}
        pos = {}
        for op in self.ops:
            l = per_stream.setdefault(op.stream, [])
            pos[op.seq] = len(l)
            l.append(op)
        per_eng_issue = {}
        ipos = {}
        for op in self.ops:
            l = per_eng_issue.setdefault(op.engine, [])
            ipos[op.seq] = len(l)
            l.append(op)
        val2pos = {}
        running = {}
        for op in self.ops:
            if op.inc:
                v = running.get(op.stream, 0) + op.inc_val
                running[op.stream] = v
                val2pos[(op.stream, v)] = pos[op.seq]
        clocks = {}
        prev_eng_clock = {}
        for op in self.ops:
            c = dict(prev_eng_clock.get(op.engine, {}))
            for (stream, value) in op.waits:
                p = val2pos.get((stream, value))
                assert p is not None, f"wait on missing value {stream} {value}"
                target = per_stream[stream][p]
                tc = clocks[target.seq]
                for k, v in tc.items():
                    if v > c.get(k, -1):
                        c[k] = v
                if p + 1 > c.get(stream, 0):
                    c[stream] = p + 1
            clocks[op.seq] = c
            nxt = dict(c)
            if op.stream in ENG:
                if pos[op.seq] + 1 > nxt.get(op.stream, 0):
                    nxt[op.stream] = pos[op.seq] + 1
            prev_eng_clock[op.engine] = nxt

        def happened(a, b):
            if a.stream == b.stream and a.stream in ENG:
                return pos[a.seq] < pos[b.seq]
            if a.stream == b.stream:
                return pos[a.seq] < pos[b.seq] and a.engine == b.engine
            return clocks[b.seq].get(a.stream, 0) >= pos[a.seq] + 1

        key_state = {}
        pages = {}
        def overlapping(sp, s, e):
            out = set()
            for pg in range(s // PAGE, (e - 1) // PAGE + 1):
                for k in pages.get((sp, pg), ()):
                    if k[1] < e and s < k[2]:
                        out.add(k)
            return out
        def addkey(key):
            if key not in key_state:
                key_state[key] = {"writer": None, "readers": {}}
                for pg in range(key[1] // PAGE, (key[2] - 1) // PAGE + 1):
                    pages.setdefault((key[0], pg), set()).add(key)
            return key_state[key]
        def delkey(key):
            key_state.pop(key, None)
            for pg in range(key[1] // PAGE, (key[2] - 1) // PAGE + 1):
                s = pages.get((key[0], pg))
                if s is not None:
                    s.discard(key)
        bad = []
        for op in self.ops:
            for (sp, a, b) in op.reads:
                for k in overlapping(sp, a, b):
                    w = key_state[k]["writer"]
                    if w is not None and w.seq != op.seq and not happened(w, op):
                        bad.append(("RAW", w.seq, w.tag, op.seq, op.tag, k))
                st = addkey((sp, a, b))
                st["readers"][op.stream] = op
            for (sp, a, b) in op.writes:
                key = (sp, a, b)
                for k in overlapping(sp, a, b):
                    ks = key_state[k]
                    w = ks["writer"]
                    if w is not None and w.seq != op.seq and not happened(w, op):
                        bad.append(("WAW", w.seq, w.tag, op.seq, op.tag, k))
                    for rd in ks["readers"].values():
                        if rd.seq != op.seq and not happened(rd, op):
                            bad.append(("WAR", rd.seq, rd.tag, op.seq, op.tag, k))
                    if k != key:
                        frag = {"writer": ks["writer"],
                                "readers": dict(ks["readers"])}
                        delkey(k)
                        if k[1] < a:
                            fs = addkey((sp, k[1], a))
                            fs.update({"writer": frag["writer"],
                                       "readers": dict(frag["readers"])})
                        if k[2] > b:
                            fs = addkey((sp, b, k[2]))
                            fs.update({"writer": frag["writer"],
                                       "readers": dict(frag["readers"])})
                st = addkey(key)
                st["writer"] = op
                st["readers"] = {}
        return bad

    def emit(self, nc):
        per_eng = {e: [] for e in ENG}
        for op in self.ops:
            per_eng[op.engine].append(op)
        engmap = {"t": nc.tensor, "v": nc.vector, "s": nc.scalar,
                  "g": nc.gpsimd, "y": nc.sync}
        with nc.Block() as blk:
            def make_body(eng_name):
                def body(eng):
                    for op in per_eng[eng_name]:
                        for stream, val in op.waits:
                            eng.wait_ge(self.sems[stream], val)
                        if op.pre_drain:
                            eng.drain()
                        ins = op.fn(eng)
                        if op.inc:
                            assert ins is not None, f"op {op.tag} needs inc but fn returned None"
                            ins.then_inc(self.sems[op.stream], op.inc_val)
                    if eng_name == "y":
                        for st, cnt in sorted(self.final_counts.items()):
                            if st.startswith("d_") and cnt > 0:
                                eng.wait_ge(self.sems[st], cnt)
                return body
            blk.tensor(make_body("t"))
            blk.vector(make_body("v"))
            blk.scalar(make_body("s"))
            blk.gpsimd(make_body("g"))
            blk.sync(make_body("y"))



class Arena:
    def __init__(self, space, nbytes):
        self.space = space
        self.free = [(0, nbytes)]
        self.hwm = 0

    def alloc(self, nbytes, align=64):
        nbytes += PAD_BYTES
        for i, (a, b) in enumerate(self.free):
            s = (a + align - 1) // align * align
            if s + nbytes <= b:
                if s > a:
                    self.free[i] = (a, s)
                    self.free.insert(i + 1, (s + nbytes, b))
                else:
                    self.free[i] = (s + nbytes, b)
                if self.free[i][0] >= self.free[i][1]:
                    del self.free[i]
                self.hwm = max(self.hwm, s + nbytes)
                return (s, s + nbytes)
        raise MemoryError(f"{self.space}: cannot alloc {nbytes}")

    def free_(self, reg):
        a, b = reg
        self.free.append((a, b))
        self.free.sort()
        merged = []
        for x in self.free:
            if merged and merged[-1][1] >= x[0]:
                merged[-1] = (merged[-1][0], max(merged[-1][1], x[1]))
            else:
                merged.append(list(x))
        self.free = [tuple(x) for x in merged]


class Tile:
    __slots__ = ("kb", "arena", "reg", "ap_full", "dtype", "parts", "n", "esz")

    def __init__(self, kb, arena_name, parts, n, dtype):
        self.kb = kb
        self.dtype = dtype
        self.parts = parts
        self.n = n
        self.esz = 2 if dtype == BF16 else 4
        self.arena = kb.arenas[arena_name]
        self.reg = self.arena.alloc(n * self.esz)
        backing = kb.backing[arena_name]
        off = self.reg[0] // self.esz
        self.ap_full = backing.ap()[0:parts, off:off + n]

    def __getitem__(self, idx):
        return self.ap_full[idx]

    @property
    def ap(self):
        return self.ap_full

    def r(self, a=None, b=None):
        if a is None:
            return (self.arena.space, self.reg[0], self.reg[1])
        return (self.arena.space, self.reg[0] + a * self.esz, self.reg[0] + b * self.esz)

    def free(self):
        self.arena.free_(self.reg)



class KB:
    SB_B = 150 * 1024
    SB_F = 48 * 1024

    def __init__(self):
        nc = bass.Bass("TRN2", target_bir_lowering=False)
        self.nc = nc
        self.rec = Rec()
        self.arenas = {
            "sbB": Arena("sbB", self.SB_B),
            "sbF": Arena("sbF", self.SB_F),
        }
        self.backing = {
            "sbB": nc.alloc_sbuf_tensor("arenaB", [128, self.SB_B // 2], BF16),
            "sbF": nc.alloc_sbuf_tensor("arenaF", [128, self.SB_F // 4], F32),
        }
        self.psF = nc.alloc_psum_tensor("psF", [128, 3584], F32)
        self.psB = nc.alloc_psum_tensor("psB", [128, 1024], BF16)
        self.junk = nc.alloc_sbuf_tensor("junkt", [1, 4], BF16).ap()
        self.hbm = {}
        self.consts = {}
        self.const_tiles = {}
        self._hostc = host_consts()

    def hbm_tensor(self, name, shape, dtype, kind="Internal"):
        h = self.nc.dram_tensor(name, list(shape), dtype, kind=kind)
        ap = h.ap()
        esz = 2 if dtype == BF16 else 4
        rowbytes = int(np.prod(shape[1:])) * esz
        self.hbm[name] = (ap, shape, dtype, rowbytes)
        return ap

    def hr(self, name, r0=None, r1=None):
        ap, shape, dtype, rowbytes = self.hbm[name]
        if r0 is None:
            return ("h_" + name, 0, shape[0] * rowbytes)
        return ("h_" + name, r0 * rowbytes, r1 * rowbytes)

    def bankF(self, i, n=512, off=0):
        return self.psF.ap()[:, i * 512 + off:i * 512 + off + n]

    def bankF_r(self, i, n=512, off=0):
        return ("psF", (i * 512 + off) * 4, (i * 512 + off + n) * 4)

    def bankB(self, off, n):
        return self.psB.ap()[:, off:off + n]

    def bankB_r(self, off, n):
        return ("psB", off * 2, (off + n) * 2)

    def const(self, name, arr):
        if name in self.const_tiles:
            return self.const_tiles[name]
        dtype = BF16 if arr.dtype == BF else F32
        parts, n = arr.shape
        ap = self.hbm_tensor("c_" + name, arr.shape, dtype, kind="ExternalInput")
        self.consts["c_" + name] = arr
        t = Tile(self, "sbB" if dtype == BF16 else "sbF", parts, n, dtype)
        self.rec.dma("y", t[0:parts, :], ap, [self.hr("c_" + name)], [t.r()],
                     tag="const_" + name)
        self.const_tiles[name] = t
        return t

    def op(self, engine, fn, reads, writes, tag="", pre_drain=False):
        return self.rec.add(engine, fn, reads, writes, tag=tag,
                            pre_drain=pre_drain)



def host_consts():
    c = {}
    c["ident"] = np.eye(128, dtype=BF)
    c["ones_col"] = np.full((128, 1), 1.0, dtype=BF)
    c["ones_mat"] = np.ones((128, 128), dtype=BF)
    c["ones_mat_scaled"] = np.full((128, 128), 1.0 / D, dtype=BF)
    c["ones_row_f"] = np.ones((1, 128), dtype=np.float32)
    for r in range(4):
        i = np.arange(128)[:, None]
        j = np.arange(512)[None, :]
        c["mask512_%d" % r] = np.where(j <= 128 * r + i, 0.0, NEG).astype(np.float32)
    mm = np.tril(np.ones((128, 128), dtype=np.float32), 0).T
    c["mask_mul"] = mm.astype(BF)
    def rope_tables(hd, theta, mixed):
        half = hd // 2
        inv = 1.0 / theta ** (np.arange(0, half, dtype=np.float64) * 2 / hd)
        tpos = np.arange(T, dtype=np.float64)
        ang = np.outer(inv, tpos)
        cos = np.cos(ang)
        sin = np.sin(ang)
        nheads = 128 // hd
        C = np.ones((128, T), dtype=np.float64)
        S = np.zeros((128, T), dtype=np.float64)
        P = np.zeros((128, 128), dtype=np.float64)
        for h in range(nheads):
            base = h * hd
            if mixed and h == 0:
                continue
            C[base:base + half] = cos
            C[base + half:base + hd] = cos
            S[base:base + half] = sin
            S[base + half:base + hd] = sin
            for m_ in range(half):
                P[base + half + m_, base + m_] = -1.0
                P[base + m_, base + half + m_] = 1.0
        return P.astype(BF), C.astype(BF), S.astype(BF)

    for name, (hd, theta, mixed) in {
        "l0": (64, 500000.0, False),
        "l0f": (64, 10000.0, True),
        "l1": (128, 10000.0, False),
    }.items():
        if name == "l0f":
            Pm, C, S = rope_tables(64, theta, False)
            C[0:64] = 1.0
            S[0:64] = 0.0
            Pm[:, 0:64] = 0.0
            Pm[0:64, :] = 0.0
            c["perm_" + name], c["cos_" + name], c["sin_" + name] = Pm, C, S
        else:
            Pm, C, S = rope_tables(hd, theta, mixed)
            c["perm_" + name], c["cos_" + name], c["sin_" + name] = Pm, C, S
    return c



def load_feat_cols(kb, tile, hbm_name, nrows, c0, c1, cast=False, tag=""):
    ap, shape, dtype, rowb = kb.hbm[hbm_name]
    nch = nrows // 128
    w = c1 - c0
    src = ap[0:nrows, c0:c1].rearrange("(ch p) t -> p ch t", p=128)
    dst = tile.ap.rearrange("p (ch t) -> p ch t", ch=nch)
    eng = "g" if cast else "y"
    return kb.rec.dma(eng, dst, src, [kb.hr(hbm_name)], [tile.r()], tag=tag or ("ld_" + hbm_name))


def load_rows(kb, tile, hbm_name, r0, nrows, c0=None, c1=None, cast=False, tag="", eng=None):
    ap, shape, dtype, rowb = kb.hbm[hbm_name]
    if c0 is None:
        c0, c1 = 0, shape[1]
    src = ap[r0:r0 + nrows, c0:c1]
    dst = tile[0:nrows, 0:c1 - c0]
    if eng is None:
        eng = "g" if cast else "y"
    return kb.rec.dma(eng, dst, src, [kb.hr(hbm_name, r0, r0 + nrows)], [tile.r()],
                      tag=tag or ("ldr_" + hbm_name))


def store_rows(kb, hbm_name, r0, nrows, tile, c0=None, c1=None, eng="y", tag=""):
    ap, shape, dtype, rowb = kb.hbm[hbm_name]
    if c0 is None:
        c0, c1 = 0, shape[1]
    dst = ap[r0:r0 + nrows, c0:c1]
    src = tile[0:nrows, 0:c1 - c0]
    return kb.rec.dma(eng, dst, src, [tile.r()], [kb.hr(hbm_name, r0, r0 + nrows)],
                      tag=tag or ("st_" + hbm_name))


def store_feat_cols(kb, hbm_name, c0, c1, tile, nrows, eng="y", tag=""):
    ap, shape, dtype, rowb = kb.hbm[hbm_name]
    nch = nrows // 128
    dst = ap[0:nrows, c0:c1].rearrange("(ch p) t -> p ch t", p=128)
    src = tile.ap.rearrange("p (ch t) -> p ch t", ch=nch)
    return kb.rec.dma(eng, dst, src, [tile.r()], [kb.hr(hbm_name)], tag=tag or ("stc_" + hbm_name))


def load_wcols(kb, tile, wname, K, c0, ncols, tag=""):
    ap, shape, dtype, rowb = kb.hbm[wname]
    nkc = K // 128
    src = ap[0:K, c0:c0 + ncols].rearrange("(kc p) c -> p kc c", p=128)
    dst = tile.ap.rearrange("p (kc c) -> p kc c", kc=nkc)
    return kb.rec.dma("g", dst, src, [kb.hr(wname)], [tile.r()], tag=tag or ("ldw_" + wname))


def load_colvec(kb, tile, name, n, tag=""):
    ap, shape, dtype, rowb = kb.hbm[name]
    dst = tile[0:128, 0:n // 128]
    return kb.rec.dma("y", dst, ap, [kb.hr(name)], [tile.r()], tag=tag or ("ldv_" + name))




def load_feat_cols(kb, tile, hbm_name, nrows, c0, c1, cast=False, tag=""):
    ap, shape, dtype, rowb = kb.hbm[hbm_name]
    nch = nrows // 128
    src = ap[0:nrows, c0:c1].rearrange("(ch p) t -> p ch t", p=128)
    dst = tile.ap.rearrange("p (ch t) -> p ch t", ch=nch)
    eng = "g" if cast else "y"
    return kb.rec.dma(eng, dst, src, [kb.hr(hbm_name)], [tile.r()], tag=tag or ("ld_" + hbm_name))


def load_rows(kb, tile, hbm_name, r0, nrows, c0=None, c1=None, cast=False, tag="", eng=None):
    ap, shape, dtype, rowb = kb.hbm[hbm_name]
    if c0 is None:
        c0, c1 = 0, shape[1]
    src = ap[r0:r0 + nrows, c0:c1]
    dst = tile[0:nrows, 0:c1 - c0]
    if eng is None:
        eng = "g" if cast else "y"
    return kb.rec.dma(eng, dst, src, [kb.hr(hbm_name, r0, r0 + nrows)], [tile.r()],
                      tag=tag or ("ldr_" + hbm_name))


def store_rows(kb, hbm_name, r0, nrows, tile, c0=None, c1=None, eng="y", tag="",
               tc0=None):
    ap, shape, dtype, rowb = kb.hbm[hbm_name]
    if c0 is None:
        c0, c1 = 0, shape[1]
    if tc0 is None:
        tc0 = c0
    dst = ap[r0:r0 + nrows, c0:c1]
    src = tile[0:nrows, tc0:tc0 + (c1 - c0)]
    return kb.rec.dma(eng, dst, src, [tile.r()], [kb.hr(hbm_name, r0, r0 + nrows)],
                      tag=tag or ("st_" + hbm_name))


def store_feat_cols(kb, hbm_name, c0, c1, tile, nrows, eng="y", tag=""):
    ap, shape, dtype, rowb = kb.hbm[hbm_name]
    nch = nrows // 128
    dst = ap[0:nrows, c0:c1].rearrange("(ch p) t -> p ch t", p=128)
    src = tile.ap.rearrange("p (ch t) -> p ch t", ch=nch)
    return kb.rec.dma(eng, dst, src, [tile.r()], [kb.hr(hbm_name)], tag=tag or ("stc_" + hbm_name))


def load_wcols(kb, tile, wname, K, c0, ncols, tag=""):
    ap, shape, dtype, rowb = kb.hbm[wname]
    nkc = K // 128
    src = ap[0:K, c0:c0 + ncols].rearrange("(kc p) c -> p kc c", p=128)
    dst = tile.ap.rearrange("p (kc c) -> p kc c", kc=nkc)
    return kb.rec.dma("g", dst, src, [kb.hr(wname)], [tile.r()], tag=tag or ("ldw_" + wname))


def load_colvec(kb, tile, name, n, tag=""):
    ap, shape, dtype, rowb = kb.hbm[name]
    dst = tile[0:128, 0:n // 128]
    return kb.rec.dma("y", dst, ap, [kb.hr(name)], [tile.r()], tag=tag or ("ldv_" + name))



def ln_phase(kb, src, dst, mode="ln", eps=1e-5, gamma_tile=None, base=None,
             dst_dtype=BF16, src_is_bf=False, tag="ln"):
    onesm = kb.const("ones_mat_scaled", kb._hostc["ones_mat_scaled"])
    epsn = f"eps_{eps:g}"
    eps_t = kb.const(epsn, np.full((128, 1), eps, dtype=np.float32))

    xb = [Tile(kb, "sbB", 128, T, BF16) for _ in range(NCH)]
    sq = [Tile(kb, "sbB", 128, 512, BF16) for _ in range(2)]
    mean_sb = Tile(kb, "sbF", 128, 512, F32)
    var_sb = Tile(kb, "sbF", 128, 512, F32)
    rstd_sb = Tile(kb, "sbF", 128, 512, F32)
    tmpa = [Tile(kb, "sbF", 128, 512, F32) for _ in range(2)]
    base_t = Tile(kb, "sbF", 128, 512, F32) if base is not None else None
    out_t = [Tile(kb, "sbF" if dst_dtype == F32 else "sbB", 128, 512, dst_dtype)
             for _ in range(2)]

    for c in range(NCH):
        load_rows(kb, xb[c], src, c * 128, 128, cast=not src_is_bf, tag=tag + "ld")

    for ts in range(4):
        t0 = ts * 512
        for c in range(NCH):
            s = sq[c % 2]
            kb.op("s", lambda e, o=s.ap, xc=xb[c], t0=t0: e.activation(
                o, xc[:, t0:t0 + 512], AF.Square),
                [xb[c].r(t0, t0 + 512)], [s.r()], tag=tag + "sq")
            if mode == "ln":
                kb.op("t", lambda e, xc=xb[c], t0=t0, c=c: e.matmul(
                    kb.bankF(5), onesm[0:128, 0:128], xc[:, t0:t0 + 512],
                    start=(c == 0), stop=(c == NCH - 1)),
                    [xb[c].r(t0, t0 + 512), onesm.r()], [kb.bankF_r(5)], tag=tag + "sx")
            kb.op("t", lambda e, s=s, c=c: e.matmul(
                kb.bankF(6), onesm[0:128, 0:128], s[:, 0:512],
                start=(c == 0), stop=(c == NCH - 1)),
                [s.r()], [kb.bankF_r(6)], tag=tag + "sq2")
        if mode == "ln":
            kb.op("v", lambda e: e.tensor_copy(mean_sb.ap, kb.bankF(5)),
                  [kb.bankF_r(5)], [mean_sb.r()], tag=tag + "evm")
            kb.op("v", lambda e: e.tensor_mul(var_sb.ap, mean_sb.ap, mean_sb.ap),
                  [mean_sb.r()], [var_sb.r()], tag=tag + "m2", pre_drain=True)
            kb.op("v", lambda e: e.tensor_sub(var_sb.ap, kb.bankF(6), var_sb.ap),
                  [kb.bankF_r(6), var_sb.r()], [var_sb.r()], tag=tag + "var")
        else:
            kb.op("v", lambda e: e.tensor_copy(var_sb.ap, kb.bankF(6)),
                  [kb.bankF_r(6)], [var_sb.r()], tag=tag + "ev2")
        kb.op("s", lambda e: e.activation(var_sb.ap, var_sb.ap, AF.Ln,
                                          bias=eps_t[0:128, 0:1]),
              [var_sb.r(), eps_t.r()], [var_sb.r()], tag=tag + "log")
        kb.op("s", lambda e: e.activation(rstd_sb.ap, var_sb.ap, AF.Exp, scale=-0.5),
              [var_sb.r()], [rstd_sb.r()], tag=tag + "rst")
        for c in range(NCH):
            tm = tmpa[c % 2]
            o = out_t[c % 2]
            if mode == "ln":
                kb.op("v", lambda e, xc=xb[c], t0=t0, tm=tm: e.tensor_sub(
                    tm.ap, xc[:, t0:t0 + 512], mean_sb.ap),
                    [xb[c].r(t0, t0 + 512), mean_sb.r()], [tm.r()], tag=tag + "sub")
            else:
                kb.op("v", lambda e, xc=xb[c], t0=t0, tm=tm, c=c: e.tensor_scalar(
                    tm.ap, xc[:, t0:t0 + 512], gamma_tile[0:128, c:c + 1], None,
                    op0=ALU.mult),
                    [xb[c].r(t0, t0 + 512), gamma_tile.r()], [tm.r()], tag=tag + "gam")
            if base is None:
                kb.op("v", lambda e, tm=tm, o=o: e.tensor_mul(o.ap, tm.ap, rstd_sb.ap),
                      [tm.r(), rstd_sb.r()], [o.r()], tag=tag + "mul")
            else:
                kb.op("v", lambda e, tm=tm: e.tensor_mul(tm.ap, tm.ap, rstd_sb.ap),
                      [tm.r(), rstd_sb.r()], [tm.r()], tag=tag + "mul")
                load_rows(kb, base_t, base, c * 128, 128, t0, t0 + 512,
                          tag=tag + "bas", eng="s")
                kb.op("v", lambda e, tm=tm, o=o: e.tensor_add(
                    o.ap, base_t[0:128, 0:512], tm.ap),
                    [tm.r(), base_t.r()], [o.r()], tag=tag + "add")
            store_rows(kb, dst, c * 128, 128, o, t0, t0 + 512, eng="s",
                       tag=tag + "st", tc0=0)
    for t_ in xb + sq + [mean_sb, var_sb, rstd_sb] + tmpa + out_t:
        t_.free()
    if base_t is not None:
        base_t.free()



def proj_phase(kb, nchunks, parts, dst=None, dst_tiles=None, rope=None,
               base=None, scale=1.0, dst_dtype=BF16, tag="pj", dbg_xr=None):
    nparts = len(parts)
    wt = []
    for p in parts:
        nkc = p["K"] // 128
        wt.append([Tile(kb, "sbB", 128, nkc * p["cols"], BF16) for _ in range(2)])

    def fetch(fc):
        for pi, p in enumerate(parts):
            tl = wt[pi][fc % 2]
            load_wcols(kb, tl, p["w"], p["K"], fc * p["cols"], p["cols"],
                       tag=tag + "w%d" % pi)
            if DMA_TOUCH:
                kb.op("g", lambda e, tl=tl: e.tensor_copy(
                    kb.junk[0:1, 0:1], tl[0:1, 0:1]),
                    [tl.r()], [tl.r(), ("junk", 0, 4)], tag=tag + "tch")

    if rope is not None:
        var, do_elu = rope
        permt = kb.const("perm_" + var, kb._hostc["perm_" + var])
        cost = kb.const("cos_" + var, kb._hostc["cos_" + var])
        sint = kb.const("sin_" + var, kb._hostc["sin_" + var])
        xraw = [Tile(kb, "sbB", 128, 512, BF16) for _ in range(2)]
        xsw = [Tile(kb, "sbB", 128, 512, BF16) for _ in range(2)]
        tr1 = [Tile(kb, "sbB", 128, 512, BF16) for _ in range(2)]
        tr2 = [Tile(kb, "sbB", 128, 512, BF16) for _ in range(2)]
    base_t = Tile(kb, "sbF", 128, T, F32) if base is not None else None
    outs = None
    if dst_tiles is None:
        outs = [Tile(kb, "sbF" if dst_dtype == F32 else "sbB", 128, T, dst_dtype)
                for _ in range(2)]

    fetch(0)
    for fc in range(nchunks):
        if PREFETCH and fc + 1 < nchunks:
            fetch(fc + 1)
        if not PREFETCH and fc > 0:
            fetch(fc)
        o = dst_tiles[fc] if dst_tiles is not None else outs[fc % 2]
        if base is not None:
            load_rows(kb, base_t, base, fc * 128, 128, tag=tag + "bas", eng="s")
        for ts in range(4):
            t0 = ts * 512
            bank = (fc * 4 + ts) % 3
            def mmg(e, fc=fc, t0=t0, bank=bank):
                ins = None
                for pi, p in enumerate(parts):
                    nkc = p["K"] // 128
                    w = wt[pi][fc % 2]
                    r0, cols = p["row0"], p["cols"]
                    for kc in range(nkc):
                        ins = e.matmul(
                            kb.bankF(bank)[r0:r0 + cols, :],
                            w[:, kc * cols:(kc + 1) * cols],
                            p["src"][kc][:, t0:t0 + 512],
                            start=(kc == 0), stop=(kc == nkc - 1))
                if MM_DRAIN_GUARD:
                    p0 = parts[0]
                    ins = e.matmul(kb.bankF(6)[0:1, 0:64],
                                   wt[0][fc % 2][:, 0:1],
                                   p0["src"][0][:, t0:t0 + 64],
                                   start=True, stop=True)
                return ins
            reads = [wt[pi][fc % 2].r() for pi in range(nparts)]
            for p in parts:
                for kc in range(p["K"] // 128):
                    reads.append(p["src"][kc].r(t0, t0 + 512))
            kb.op("t", mmg, reads, [kb.bankF_r(bank)], tag=tag + "mm")

            if rope is None and base is None:
                kb.op("s", lambda e, o=o, t0=t0, bank=bank: e.activation(
                    o[:, t0:t0 + 512], kb.bankF(bank), AF.Identity, scale=scale),
                    [kb.bankF_r(bank)], [o.r(t0, t0 + 512)], tag=tag + "ev")
            elif base is not None:
                kb.op("v", lambda e, o=o, t0=t0, bank=bank: e.tensor_add(
                    o[:, t0:t0 + 512], kb.bankF(bank), base_t[:, t0:t0 + 512]),
                    [kb.bankF_r(bank), base_t.r(t0, t0 + 512)], [o.r(t0, t0 + 512)],
                    tag=tag + "evb")
            else:
                xr = xraw[ts % 2]
                xs = xsw[ts % 2]
                u1 = tr1[ts % 2]
                u2 = tr2[ts % 2]
                kb.op("s", lambda e, xr=xr, bank=bank: e.activation(
                    xr.ap, kb.bankF(bank), AF.Identity),
                    [kb.bankF_r(bank)], [xr.r()], tag=tag + "rpe")
                if dbg_xr is not None:
                    store_rows(kb, dbg_xr, (fc * 4 + ts) * 128, 128, xr,
                               c0=0, c1=512, tag=tag + "dbg")
                kb.op("t", lambda e, xr=xr: e.matmul(
                    kb.bankF(3), permt[0:128, 0:128], xr.ap, start=True, stop=True),
                    [xr.r(), permt.r()], [kb.bankF_r(3)], tag=tag + "rpm")
                kb.op("s", lambda e, xs=xs: e.activation(
                    xs.ap, kb.bankF(3), AF.Identity),
                    [kb.bankF_r(3)], [xs.r()], tag=tag + "rpw")
                kb.op("v", lambda e, xr=xr, u1=u1, t0=t0: e.tensor_mul(
                    u1.ap, xr.ap, cost[0:128, t0:t0 + 512]),
                    [xr.r(), cost.r()], [u1.r()], tag=tag + "rc")
                kb.op("v", lambda e, xs=xs, u2=u2, t0=t0: e.tensor_mul(
                    u2.ap, xs.ap, sint[0:128, t0:t0 + 512]),
                    [xs.r(), sint.r()], [u2.r()], tag=tag + "rs")
                if not do_elu:
                    kb.op("v", lambda e, u1=u1, u2=u2, o=o, t0=t0: e.tensor_add(
                        o[:, t0:t0 + 512], u1.ap, u2.ap),
                        [u1.r(), u2.r()], [o.r(t0, t0 + 512)], tag=tag + "ra")
                else:
                    kb.op("v", lambda e, u1=u1, u2=u2: e.tensor_add(
                        u1.ap, u1.ap, u2.ap),
                        [u1.r(), u2.r()], [u1.r()], tag=tag + "ra2")
                    kb.op("v", lambda e, u1=u1, u2=u2: e.tensor_scalar_min(
                        u2.ap, u1.ap, 0.0),
                        [u1.r()], [u2.r()], tag=tag + "emn")
                    kb.op("s", lambda e, u2=u2: e.activation(u2.ap, u2.ap, AF.Exp),
                          [u2.r()], [u2.r()], tag=tag + "eex")
                    kb.op("v", lambda e, u1=u1: e.tensor_scalar_max(
                        u1.ap, u1.ap, 0.0),
                        [u1.r()], [u1.r()], tag=tag + "erl")
                    kb.op("v", lambda e, u1=u1, u2=u2, o=o, t0=t0: e.tensor_add(
                        o[:, t0:t0 + 512], u1.ap, u2.ap),
                        [u1.r(), u2.r()], [o.r(t0, t0 + 512)], tag=tag + "eadd")
        if dst is not None:
            store_rows(kb, dst, fc * 128, 128, o, tag=tag + "st")
    for pr in wt:
        for t_ in pr:
            t_.free()
    if rope is not None:
        for t_ in xraw + xsw + tr1 + tr2:
            t_.free()
    if base_t is not None:
        base_t.free()
    if dst_tiles is None:
        for t_ in outs:
            t_.free()



def flash_phase(kb, nH, hd, q_hbm, out_hbm, scale, heads, k_of, v_of, tag="fa",
                dbg=None):
    ident = kb.const("ident", kb._hostc["ident"])
    onesm = kb.const("ones_mat", kb._hostc["ones_mat"])
    masks = [kb.const("mask512_%d" % r, kb._hostc["mask512_%d" % r]) for r in range(4)]
    qch = Tile(kb, "sbB", 128, T, BF16)
    sc = Tile(kb, "sbF", 128, T, F32)
    pexp = Tile(kb, "sbB", 128, T, BF16)
    PT = Tile(kb, "sbB", 128, 16 * 512, BF16)
    bias = Tile(kb, "sbF", 128, 1, F32)
    lsum = Tile(kb, "sbF", 128, 1, F32)
    rl = Tile(kb, "sbF", 128, 1, F32)
    rl512 = Tile(kb, "sbF", 128, 512, F32)
    OT = Tile(kb, "sbB", 128, T, BF16)
    if dbg is not None:
        dbg.update(dict(sc=sc, pexp=pexp, PT=PT, bias=bias, lsum=lsum, rl=rl))
    per_chunk = 128 // hd

    nchunks = nH // per_chunk
    for c in sorted(set(h["c"] for h in heads)):
        load_rows(kb, qch, q_hbm, c * 128, 128, tag=tag + "q")
        hh = [h for h in heads if h["c"] == c]
        for h in hh:
            if h.get("load"):
                h["load"](kb)
        for h in hh:
            base = h["base"]
            ktile, krow = k_of(h)
            vslab, voff = v_of(h)
            for grp in range(4):
                nk5 = grp + 1
                nkt = 4 * nk5
                nk = nk5 * 512
                for qi in range(4):
                    qt = grp * 4 + qi
                    q0 = qt * 128
                    def smm(e, base=base, q0=q0, nk5=nk5, ktile=ktile, krow=krow):
                        ins = None
                        for kt in range(nk5):
                            ins = e.matmul(
                                kb.bankF(kt)[0:128, :],
                                qch[base:base + hd, q0:q0 + 128],
                                ktile[krow:krow + hd, kt * 512:(kt + 1) * 512],
                                start=True, stop=True)
                        return ins
                    kb.op("t", smm, [qch.r(), ktile.r()],
                          [kb.bankF_r(kt) for kt in range(nk5)], tag=tag + "s")
                    kb.op("s", lambda e, nk=nk: e.activation(
                        sc[:, 0:nk], kb.psF.ap()[:, 0:nk], AF.Identity, scale=scale),
                        [kb.bankF_r(kt) for kt in range(nk5)], [sc.r(0, nk)],
                        tag=tag + "ev")
                    mk = masks[qi]
                    kb.op("g", lambda e, nk=nk, mk=mk: e.tensor_add(
                        sc[:, nk - 512:nk], sc[:, nk - 512:nk], mk.ap),
                        [sc.r(nk - 512, nk), mk.r()], [sc.r(nk - 512, nk)],
                        tag=tag + "mk")
                    kb.op("v", lambda e, nk=nk: e.reduce_max(
                        bias.ap, sc[:, 0:nk], axis=AX.X, negate=True),
                        [sc.r(0, nk)], [bias.r()], tag=tag + "mx")
                    kb.op("s", lambda e, nk=nk: e.activation(
                        pexp[:, 0:nk], sc[:, 0:nk], AF.Exp, bias=bias[0:128, 0:1]),
                        [sc.r(0, nk), bias.r()], [pexp.r(0, nk)],
                        tag=tag + "ex")
                    for jb in range(0, nkt, 8):
                        nb = min(8, nkt - jb)
                        def trf(e, jb=jb, nb=nb):
                            ins = None
                            for j in range(nb):
                                ins = e.transpose(
                                    kb.bankB(j * 128, 128),
                                    pexp[:, (jb + j) * 128:(jb + j + 1) * 128],
                                    ident[0:128, 0:128])
                            return ins
                        kb.op("t", trf, [pexp.r(0, nk), ident.r()],
                              [kb.bankB_r(0, nb * 128)], tag=tag + "tr")
                        ptv = PT.ap.rearrange("p (j q) -> p j q", q=512)
                        pbv = kb.psB.ap()[:, 0:nb * 128].rearrange(
                            "p (j q) -> p j q", q=128)
                        kb.op("v", lambda e, jb=jb, nb=nb, qi=qi, ptv=ptv, pbv=pbv:
                              e.tensor_copy(
                                  ptv[:, jb:jb + nb, qi * 128:(qi + 1) * 128], pbv),
                              [kb.bankB_r(0, nb * 128)],
                              [PT.r(jb * 512, (jb + nb) * 512)], tag=tag + "pt")
                def pvf(e, nkt=nkt, vslab=vslab, voff=voff):
                    ins = None
                    for j in range(nkt):
                        ins = e.matmul(
                            kb.bankF(4)[0:hd, :],
                            vslab[:, j * 128 + voff:j * 128 + voff + hd],
                            PT[:, j * 512:(j + 1) * 512],
                            start=(j == 0), stop=(j == nkt - 1))
                    return ins
                kb.op("t", pvf, [vslab.r(), PT.r(0, nkt * 512)],
                      [kb.bankF_r(4)], tag=tag + "pv")
                def lmm(e, nkt=nkt):
                    ins = None
                    for j in range(nkt):
                        ins = e.matmul(kb.bankF(5), onesm[0:128, 0:128],
                                       PT[:, j * 512:(j + 1) * 512],
                                       start=(j == 0), stop=(j == nkt - 1))
                    return ins
                kb.op("t", lmm, [onesm.r(), PT.r(0, nkt * 512)],
                      [kb.bankF_r(5)], tag=tag + "lm")
                kb.op("v", lambda e: e.reciprocal(rl512.ap, kb.bankF(5)),
                      [kb.bankF_r(5)], [rl512.r()], tag=tag + "rc")
                kb.op("v", lambda e, base=base, grp=grp: e.tensor_mul(
                    OT[base:base + hd, grp * 512:(grp + 1) * 512],
                    kb.bankF(4)[0:hd, :], rl512[0:hd, 0:512]),
                    [kb.bankF_r(4), rl512.r()],
                    [OT.r(grp * 512, (grp + 1) * 512)], tag=tag + "oe")
        store_rows(kb, out_hbm, c * 128, 128, OT, tag=tag + "ot")
    if dbg is None:
        for t_ in [qch, sc, pexp, PT, bias, lsum, rl, OT]:
            t_.free()


def build_v_tm(kb, v_tiles, nslabs, tag="vtm"):
    ident = kb.const("ident", kb._hostc["ident"])
    slabs = [Tile(kb, "sbB", 128, 16 * 128, BF16) for _ in range(nslabs)]
    for a, vt in enumerate(v_tiles):
        for tb in range(0, 16, 8):
            def trf(e, vt=vt, tb=tb):
                ins = None
                for j in range(8):
                    ins = e.transpose(kb.bankB(j * 128, 128),
                                      vt[:, (tb + j) * 128:(tb + j + 1) * 128],
                                      ident[0:128, 0:128])
                return ins
            kb.op("t", trf, [vt.r(), ident.r()], [kb.bankB_r(0, 1024)], tag=tag)
            kb.op("v", lambda e, a=a, tb=tb: e.tensor_copy(
                slabs[a][:, tb * 128:(tb + 8) * 128], kb.psB.ap()[:, 0:1024]),
                [kb.bankB_r(0, 1024)], [slabs[a].r(tb * 128, (tb + 8) * 128)],
                tag=tag + "ev")
    return slabs



def qperm_l0():
    order = []
    for c in range(16):
        a, r = c // 4, c % 4
        for slot in range(2):
            h = 8 * a + 4 * slot + r
            order.extend(range(64 * h, 64 * h + 64))
    return np.array(order)


def load_act_tiles(kb, name, nrows, tag="lat"):
    tiles = [Tile(kb, "sbB", 128, T, BF16) for _ in range(nrows // 128)]
    for c, t_ in enumerate(tiles):
        load_rows(kb, t_, name, c * 128, 128, tag=tag)
    return tiles


def emit_slot1(kb):
    ln_phase(kb, "XT", "A1", mode="ln", eps=1e-5, tag="s1ln")
    Asrc = load_act_tiles(kb, "A1", D, tag="s1A")
    proj_phase(kb, 16, [dict(w="l0_wqT", K=D, src=Asrc, cols=128, row0=0)],
               dst="QT", rope=("l0", False), tag="s1q")
    ktiles = [Tile(kb, "sbB", 128, T, BF16) for _ in range(4)]
    proj_phase(kb, 4, [dict(w="l0_wkT", K=D, src=Asrc, cols=128, row0=0)],
               dst_tiles=ktiles, rope=("l0", False), tag="s1k")
    vtiles = [Tile(kb, "sbB", 128, T, BF16) for _ in range(4)]
    proj_phase(kb, 4, [dict(w="l0_wvT", K=D, src=Asrc, cols=128, row0=0)],
               dst_tiles=vtiles, tag="s1v")
    for t_ in Asrc:
        t_.free()
    vtm = build_v_tm(kb, vtiles, 4, tag="s1vt")
    heads = []
    for c in range(16):
        a = c // 4
        for slot in range(2):
            heads.append(dict(c=c, base=64 * slot, a=a, slot=slot))
    flash_phase(kb, 32, 64, "QT", "AT1", 0.125, heads,
                k_of=lambda h: (ktiles[h["a"]], 64 * h["slot"]),
                v_of=lambda h: (vtm[h["a"]], 64 * h["slot"]), tag="s1f")
    for t_ in ktiles + vtiles + vtm:
        t_.free()
    ATsrc = load_act_tiles(kb, "AT1", D, tag="s1at")
    proj_phase(kb, 16, [dict(w="l0_woT", K=D, src=ATsrc, cols=128, row0=0)],
               dst="O1", tag="s1o")
    for t_ in ATsrc:
        t_.free()
    ln_phase(kb, "O1", "R1", mode="ln", eps=1e-5, base="XT", src_is_bf=True,
             dst_dtype=F32, tag="s1r")


def host_weights_slot1(inp):
    order = qperm_l0()
    return {
        "l0_wqT": np.ascontiguousarray(inp["L0_attn_wq"].T[:, order]),
        "l0_wkT": np.ascontiguousarray(inp["L0_attn_wk"].T),
        "l0_wvT": np.ascontiguousarray(inp["L0_attn_wv"].T),
        "l0_woT": np.ascontiguousarray(inp["L0_attn_wo"].T[order, :]),
    }


def declare_slot1(kb, out_kind="Internal", final=False):
    kb.hbm_tensor("XT", [D, T], F32, kind="ExternalInput")
    for n in ("l0_wqT", "l0_woT"):
        kb.hbm_tensor(n, [D, D], F32, kind="ExternalInput")
    for n in ("l0_wkT", "l0_wvT"):
        kb.hbm_tensor(n, [D, 512], F32, kind="ExternalInput")
    kb.hbm_tensor("A1", [D, T], BF16, kind=out_kind)
    kb.hbm_tensor("QT", [D, T], BF16, kind=out_kind)
    kb.hbm_tensor("AT1", [D, T], BF16, kind=out_kind)
    kb.hbm_tensor("O1", [D, T], BF16, kind=out_kind)
    kb.hbm_tensor("R1", [D, T], F32,
                  kind="ExternalOutput" if (out_kind != "Internal" or final) else "Internal")


def finish(kb, serial=False, serial_eng=None, strict=False):
    kb.rec.finalize(kb.nc, serial=serial, serial_eng=serial_eng, strict=strict)
    kb.rec.emit(kb.nc)
    return kb.nc



def router_phase(kb, src_tiles, wname, E, bias_name=None, act="sigmoid",
                 tag="rt"):
    identf = kb.const("ident_f", np.eye(128, dtype=np.float32))
    identb = kb.const("ident", kb._hostc["ident"])
    rw = Tile(kb, "sbB", 128, NCH * 128, BF16)
    load_wcols(kb, rw, wname, D, 0, 128, tag=tag + "w")
    if bias_name is not None:
        rb = Tile(kb, "sbF", E, 1, F32)
        ap, shape, dtype, rowb = kb.hbm[bias_name]
        kb.rec.dma("y", rb[0:E, 0:1], ap.rearrange("(a b) -> a b", b=1),
                   [kb.hr(bias_name)], [rb.r()], tag=tag + "b")
    probsT = Tile(kb, "sbF", 128, T, F32)
    ptm = Tile(kb, "sbF", 128, 16 * E, F32)
    wtm = Tile(kb, "sbF", 128, 16 * E, F32)
    wtmb = Tile(kb, "sbB", 128, 16 * E, BF16)
    wT = Tile(kb, "sbB", 128, T, BF16)
    m1 = Tile(kb, "sbF", 128, 16, F32)
    m2 = Tile(kb, "sbF", 128, 16, F32)
    tA = Tile(kb, "sbF", 128, 16 * E, F32)
    tB = Tile(kb, "sbF", 128, 16 * E, F32)
    den = Tile(kb, "sbF", 128, 16, F32)
    den2 = Tile(kb, "sbF", 128, 16, F32)

    for ts in range(4):
        t0 = ts * 512
        def mmg(e, ts=ts, t0=t0):
            ins = None
            for kc in range(NCH):
                ins = e.matmul(kb.bankF(0)[0:E, :],
                               rw[:, kc * 128:kc * 128 + E],
                               src_tiles[kc][:, t0:t0 + 512],
                               start=(kc == 0), stop=(kc == NCH - 1))
            return ins
        kb.op("t", mmg, [rw.r()] + [s.r(t0, t0 + 512) for s in src_tiles],
              [kb.bankF_r(0)], tag=tag + "mm")
        if act == "sigmoid":
            kb.op("s", lambda e, t0=t0: e.activation(
                probsT[0:E, t0:t0 + 512], kb.bankF(0)[0:E, :], AF.Sigmoid,
                bias=rb[0:E, 0:1]),
                [kb.bankF_r(0), rb.r()], [probsT.r(t0, t0 + 512)], tag=tag + "sg")
        else:
            kb.op("s", lambda e, t0=t0: e.activation(
                probsT[0:E, t0:t0 + 512], kb.bankF(0)[0:E, :], AF.Identity),
                [kb.bankF_r(0)], [probsT.r(t0, t0 + 512)], tag=tag + "cp")
        for j in range(4):
            tt = ts * 4 + j
            kb.op("t", lambda e, t0=t0, j=j: e.transpose(
                kb.bankF(1)[:, 0:E], probsT[0:E, t0 + j * 128:t0 + (j + 1) * 128],
                identf[0:E, 0:E]),
                [probsT.r(t0, t0 + 512), identf.r()], [kb.bankF_r(1)],
                tag=tag + "tp")
            kb.op("v", lambda e, tt=tt: e.tensor_copy(
                ptm[:, tt * E:(tt + 1) * E], kb.bankF(1)[:, 0:E]),
                [kb.bankF_r(1)], [ptm.r(tt * E, (tt + 1) * E)], tag=tag + "pc", pre_drain=True)
    def ptile(tt):
        return ptm[:, tt * E:(tt + 1) * E], [ptm.r(tt * E, (tt + 1) * E)]

    if act == "softmax":
        for tt in range(16):
            p, pr = ptile(tt)
            kb.op("v", lambda e, p=p, tt=tt: e.reduce_max(
                m1[:, tt:tt + 1], p, axis=AX.X, negate=True),
                pr, [m1.r(tt, tt + 1)], tag=tag + "sm", pre_drain=True)
        for tt in range(16):
            p, pr = ptile(tt)
            kb.op("s", lambda e, p=p, tt=tt: e.activation(
                p, p, AF.Exp, bias=m1[:, tt:tt + 1]),
                pr + [m1.r(tt, tt + 1)], pr, tag=tag + "se")
    for tt in range(16):
        p, pr = ptile(tt)
        kb.op("v", lambda e, p=p, tt=tt: e.reduce_max(
            m1[:, tt:tt + 1], p, axis=AX.X), pr, [m1.r(tt, tt + 1)],
            tag=tag + "m1", pre_drain=True)
    for tt in range(16):
        p, pr = ptile(tt)
        kb.op("v", lambda e, p=p, tt=tt: e.tensor_scalar(
            tA[:, tt * E:(tt + 1) * E], p, m1[:, tt:tt + 1], None,
            op0=ALU.is_ge),
            pr + [m1.r(tt, tt + 1)], [tA.r(tt * E, (tt + 1) * E)],
            tag=tag + "ge", pre_drain=True)
    for tt in range(16):
        p, pr = ptile(tt)
        kb.op("v", lambda e, p=p, tt=tt: e.tensor_mul(
            tB[:, tt * E:(tt + 1) * E], tA[:, tt * E:(tt + 1) * E], p),
            [tA.r(tt * E, (tt + 1) * E)] + pr, [tB.r(tt * E, (tt + 1) * E)],
            tag=tag + "t2", pre_drain=True)
    for tt in range(16):
        p, pr = ptile(tt)
        kb.op("v", lambda e, p=p, tt=tt: e.tensor_sub(
            tB[:, tt * E:(tt + 1) * E], p, tB[:, tt * E:(tt + 1) * E]),
            pr + [tB.r(tt * E, (tt + 1) * E)], [tB.r(tt * E, (tt + 1) * E)],
            tag=tag + "p2", pre_drain=True)
    for tt in range(16):
        kb.op("v", lambda e, tt=tt: e.reduce_max(
            m2[:, tt:tt + 1], tB[:, tt * E:(tt + 1) * E], axis=AX.X),
            [tB.r(tt * E, (tt + 1) * E)], [m2.r(tt, tt + 1)], tag=tag + "m2", pre_drain=True)
    kb.op("v", lambda e: e.tensor_add(den.ap, m1.ap, m2.ap),
          [m1.r(), m2.r()], [den.r()], tag=tag + "dn", pre_drain=True)
    for tt in range(16):
        p, pr = ptile(tt)
        kb.op("v", lambda e, p=p, tt=tt: e.tensor_scalar(
            tA[:, tt * E:(tt + 1) * E], p, m2[:, tt:tt + 1], None,
            op0=ALU.is_ge),
            pr + [m2.r(tt, tt + 1)], [tA.r(tt * E, (tt + 1) * E)],
            tag=tag + "ge2", pre_drain=True)
    kb.op("v", lambda e: e.reciprocal(den2.ap, den.ap),
          [den.r()], [den2.r()], tag=tag + "dr", pre_drain=True)
    for tt in range(16):
        p, pr = ptile(tt)
        kb.op("v", lambda e, p=p, tt=tt: e.tensor_mul(
            tA[:, tt * E:(tt + 1) * E], tA[:, tt * E:(tt + 1) * E], p),
            [tA.r(tt * E, (tt + 1) * E)] + pr, [tA.r(tt * E, (tt + 1) * E)],
            tag=tag + "ws", pre_drain=True)
    for tt in range(16):
        kb.op("v", lambda e, tt=tt: e.tensor_scalar(
            wtm[:, tt * E:(tt + 1) * E], tA[:, tt * E:(tt + 1) * E],
            den2[:, tt:tt + 1], None, op0=ALU.mult),
            [tA.r(tt * E, (tt + 1) * E), den2.r(tt, tt + 1)],
            [wtm.r(tt * E, (tt + 1) * E)], tag=tag + "wn",
            pre_drain=True)
    kb.op("v", lambda e: e.tensor_copy(wtmb.ap, wtm.ap),
          [wtm.r()], [wtmb.r()], tag=tag + "wc", pre_drain=True)
    for tt in range(16):
        kb.op("t", lambda e, tt=tt: e.transpose(
            kb.bankB(0, 128)[0:E, :], wtmb[:, tt * E:(tt + 1) * E],
            identb[0:128, 0:128]),
            [wtmb.r(tt * E, (tt + 1) * E), identb.r()], [kb.bankB_r(0, 128)],
            tag=tag + "tb")
        kb.op("v", lambda e, tt=tt: e.tensor_copy(
            wT[0:E, tt * 128:(tt + 1) * 128], kb.bankB(0, 128)[0:E, :]),
            [kb.bankB_r(0, 128)], [wT.r(tt * 128, (tt + 1) * 128)], tag=tag + "wt", pre_drain=True)
    for t_ in [rw, probsT, ptm, wtm, wtmb, m1, m2, tA, tB, den, den2]:
        t_.free()
    return wT



def load_feat_rows_cols(kb, tile, hbm_name, r0, nrows, c0, c1, tag=""):
    ap, shape, dtype, rowb = kb.hbm[hbm_name]
    nch = nrows // 128
    src = ap[r0:r0 + nrows, c0:c1].rearrange("(ch p) t -> p ch t", p=128)
    dst = tile.ap.rearrange("p (ch t) -> p ch t", ch=nch)
    return kb.rec.dma("y", dst, src, [kb.hr(hbm_name, r0, r0 + nrows)], [tile.r()],
                      tag=tag)


def moe_gateup(kb, src_tiles, experts, hid_name, scaling, wT=None, nE=7,
               tag="gu"):
    rs = np.zeros((128, 9 * 128), dtype=BF)
    for e9 in range(9):
        rs[e9, e9 * 128:(e9 + 1) * 128] = 1.0
    rsel = kb.const("rowsel9", rs)
    wq = [Tile(kb, "sbB", 128, NCH * 256, BF16) for _ in range(4)]
    htile = [Tile(kb, "sbB", 128, T, BF16) for _ in range(2)]
    sg = Tile(kb, "sbB", 128, 512, BF16)

    def fetch(ei, qi, slot):
        e = experts[ei]
        load_wcols(kb, wq[slot * 2], e["gate"], D, qi * 256, 256, tag=tag + "wg")
        load_wcols(kb, wq[slot * 2 + 1], e["up"], D, qi * 256, 256, tag=tag + "wu")

    steps = [(ei, qi) for ei in range(len(experts)) for qi in range(4)]
    fetch(0, 0, 0)
    for si, (ei, qi) in enumerate(steps):
        slot = si % 2
        if si + 1 < len(steps):
            fetch(*steps[si + 1], 1 - slot)
        e = experts[ei]
        ht = htile[si % 2]
        for fc in range(2):
            for ts in range(4):
                t0 = ts * 512
                bg, bu = (0, 1) if (fc + ts) % 2 == 0 else (2, 3)
                def gmm(e_, slot=slot, fc=fc, t0=t0, bg=bg):
                    ins = None
                    for kc in range(NCH):
                        ins = e_.matmul(kb.bankF(bg),
                                        wq[slot * 2][:, kc * 256 + fc * 128:kc * 256 + (fc + 1) * 128],
                                        src_tiles[kc][:, t0:t0 + 512],
                                        start=(kc == 0), stop=(kc == NCH - 1))
                    return ins
                def umm(e_, slot=slot, fc=fc, t0=t0, bu=bu):
                    ins = None
                    for kc in range(NCH):
                        ins = e_.matmul(kb.bankF(bu),
                                        wq[slot * 2 + 1][:, kc * 256 + fc * 128:kc * 256 + (fc + 1) * 128],
                                        src_tiles[kc][:, t0:t0 + 512],
                                        start=(kc == 0), stop=(kc == NCH - 1))
                    return ins
                srcr = [s.r(t0, t0 + 512) for s in src_tiles]
                kb.op("t", gmm, [wq[slot * 2].r()] + srcr, [kb.bankF_r(bg)],
                      tag=tag + "gm")
                kb.op("t", umm, [wq[slot * 2 + 1].r()] + srcr, [kb.bankF_r(bu)],
                      tag=tag + "um")
                if e["weighted"]:
                    kb.op("t", lambda e_, ei=ei, t0=t0: e_.matmul(
                        kb.bankF(4), rsel[0:nE, ei * 128:(ei + 1) * 128],
                        wT[0:nE, t0:t0 + 512], start=True, stop=True),
                        [rsel.r(), wT.r(t0, t0 + 512)], [kb.bankF_r(4)],
                        tag=tag + "wb")
                kb.op("s", lambda e_, bg=bg: e_.activation(sg.ap, kb.bankF(bg),
                                                           AF.Silu),
                      [kb.bankF_r(bg)], [sg.r()], tag=tag + "si")
                dsl = ht[:, t0:t0 + 512]
                if scaling is not None:
                    kb.op("v", lambda e_, bu=bu, dsl=dsl: e_.scalar_tensor_tensor(
                        dsl, kb.bankF(bu), scaling, sg.ap, op0=ALU.mult,
                        op1=ALU.mult),
                        [kb.bankF_r(bu), sg.r()], [ht.r(t0, t0 + 512)],
                        tag=tag + "h")
                else:
                    kb.op("v", lambda e_, bu=bu, dsl=dsl: e_.tensor_mul(
                        dsl, kb.bankF(bu), sg.ap),
                        [kb.bankF_r(bu), sg.r()], [ht.r(t0, t0 + 512)],
                        tag=tag + "h")
                if e["weighted"]:
                    kb.op("v", lambda e_, dsl=dsl: e_.tensor_mul(
                        dsl, dsl, kb.bankF(4)),
                        [ht.r(t0, t0 + 512), kb.bankF_r(4)],
                        [ht.r(t0, t0 + 512)], tag=tag + "hw")
            row = e["hidoff"] + qi * 256 + fc * 128
            store_rows(kb, hid_name, row, 128, ht, tag=tag + "st")
    for t_ in wq + htile + [sg]:
        t_.free()


def moe_down(kb, downs, hid_name, out_name, base=None, out_dtype=BF16, tag="md"):
    nexp = len(downs)
    FG = 4
    wtiles = [Tile(kb, "sbB", 128, 8 * FG * 128, BF16) for _ in range(nexp)]
    htiles = [[Tile(kb, "sbB", 128, 8 * 512, BF16) for _ in range(3)]]
    base_t = Tile(kb, "sbF", 128, 512, F32) if base is not None else None
    out_t = [Tile(kb, "sbF" if out_dtype == F32 else "sbB", 128, 512, out_dtype)
             for _ in range(2)]
    egroups = [list(range(i, min(i + 3, nexp))) for i in range(0, nexp, 3)]

    for fg in range(4):
        f0 = fg * FG
        for ei, (wname, hidoff) in enumerate(downs):
            load_wcols(kb, wtiles[ei], wname, 1024, f0 * 128, FG * 128,
                       tag=tag + "w")
        for tc in range(4):
            t0 = tc * 512
            for gi, eg in enumerate(egroups):
                buf = htiles[0]
                for k, ei in enumerate(eg):
                    load_feat_rows_cols(kb, buf[k], hid_name,
                                        downs[ei][1], 1024, t0, t0 + 512,
                                        tag=tag + "h")
                for k, ei in enumerate(eg):
                    def dmm(e_, k=k, ei=ei, gi=gi, fg=fg):
                        ins = None
                        first = (ei == 0)
                        for fc in range(FG):
                            for kc in range(8):
                                ins = e_.matmul(
                                    kb.bankF(fc),
                                    wtiles[ei][:, (kc * FG + fc) * 128:(kc * FG + fc + 1) * 128],
                                    htiles[0][k][:, kc * 512:(kc + 1) * 512],
                                    start=(first and kc == 0),
                                    stop=(ei == nexp - 1 and kc == 7))
                        return ins
                    kb.op("t", dmm,
                          [wtiles[ei].r(), htiles[0][k].r()],
                          [kb.bankF_r(fc) for fc in range(FG)], tag=tag + "mm")
            for fc in range(FG):
                o = out_t[fc % 2]
                row = (f0 + fc) * 128
                if base is not None:
                    load_rows(kb, base_t, base, row, 128, t0, t0 + 512,
                              tag=tag + "b", eng="s")
                    kb.op("v", lambda e_, fc=fc, o=o: e_.tensor_add(
                        o.ap, kb.bankF(fc), base_t[0:128, 0:512]),
                        [kb.bankF_r(fc), base_t.r()], [o.r()], tag=tag + "ev")
                else:
                    kb.op("s", lambda e_, fc=fc, o=o: e_.activation(
                        o.ap, kb.bankF(fc), AF.Identity),
                        [kb.bankF_r(fc)], [o.r()], tag=tag + "ev")
                store_rows(kb, out_name, row, 128, o, t0, t0 + 512, tc0=0,
                           tag=tag + "st")
    for t_ in wtiles + htiles[0] + out_t:
        t_.free()
    if base_t is not None:
        base_t.free()



def declare_slot2(kb, out_kind="Internal", in_kind="Internal"):
    if "R1" not in kb.hbm:
        kb.hbm_tensor("R1", [D, T], F32, kind=in_kind)
    w = kb.hbm_tensor
    w("l0f_kv_d", [D, 512], F32, kind="ExternalInput")
    w("l0f_q_d", [D, 512], F32, kind="ExternalInput")
    w("l0f_k_u", [512, 1024], F32, kind="ExternalInput")
    w("l0f_v_u", [512, D], F32, kind="ExternalInput")
    w("l0f_q_u", [512, 1024], F32, kind="ExternalInput")
    w("l0f_rope_k", [D, 1024], F32, kind="ExternalInput")
    w("l0f_rope_q", [512, 1024], F32, kind="ExternalInput")
    w("l0f_o_w", [D, D], F32, kind="ExternalInput")
    w("l0f_ln1", [128, NCH], F32, kind="ExternalInput")
    w("l0f_ln2", [128, NCH], F32, kind="ExternalInput")
    w("l0f_router_w", [D, 128], F32, kind="ExternalInput")
    w("l0f_router_b", [7], F32, kind="ExternalInput")
    for e in range(7):
        w(f"l0f_rg{e}", [D, 1024], F32, kind="ExternalInput")
        w(f"l0f_ru{e}", [D, 1024], F32, kind="ExternalInput")
        w(f"l0f_rd{e}", [1024, D], F32, kind="ExternalInput")
    w("l0f_shg", [D, 1024], F32, kind="ExternalInput")
    w("l0f_shu", [D, 1024], F32, kind="ExternalInput")
    w("l0f_shd", [1024, D], F32, kind="ExternalInput")
    w("F2", [D, T], F32, kind=out_kind)
    w("Y2", [D, T], BF16, kind=out_kind)
    w("Q2", [D, T], BF16, kind=out_kind)
    w("K2", [D, T], BF16, kind=out_kind)
    w("V2", [D, T], BF16, kind=out_kind)
    w("AT2", [D, T], BF16, kind=out_kind)
    w("H2", [D, T], F32, kind=out_kind)
    w("Z2", [D, T], BF16, kind=out_kind)
    w("HID", [9 * 1024, T], BF16, kind=out_kind)
    w("G2", [D, T], BF16, kind=out_kind)
    kb.hbm_tensor("R2", [D, T], F32, kind=out_kind)


def host_weights_slot2(inp):
    return {
        "l0f_kv_d": inp["L0_ffn_kv_d_w"],
        "l0f_q_d": inp["L0_ffn_q_d_w"],
        "l0f_k_u": inp["L0_ffn_k_u_w"],
        "l0f_v_u": inp["L0_ffn_v_u_w"],
        "l0f_q_u": inp["L0_ffn_q_u_w"],
        "l0f_rope_k": inp["L0_ffn_rope_k_w"],
        "l0f_rope_q": inp["L0_ffn_rope_q_w"],
        "l0f_o_w": inp["L0_ffn_o_w"],
        "l0f_ln1": np.ascontiguousarray(inp["L0_ffn_ln1_w"].reshape(NCH, 128).T),
        "l0f_ln2": np.ascontiguousarray(inp["L0_ffn_ln2_w"].reshape(NCH, 128).T),
        "l0f_router_w": np.ascontiguousarray(np.pad(inp["L0_ffn_router_w"],
                                                    ((0, 0), (0, 121)))),
        "l0f_router_b": inp["L0_ffn_router_b"],
        "l0f_shg": inp["L0_ffn_sh_gate"],
        "l0f_shu": inp["L0_ffn_sh_up"],
        "l0f_shd": inp["L0_ffn_sh_down"],
        **{f"l0f_rg{e}": inp["L0_ffn_r_gate"][e] for e in range(7)},
        **{f"l0f_ru{e}": inp["L0_ffn_r_up"][e] for e in range(7)},
        **{f"l0f_rd{e}": inp["L0_ffn_r_down"][e] for e in range(7)},
    }


def emit_slot2(kb, upto=99):
    ph = [0]
    def stop():
        ph[0] += 1
        return ph[0] > upto
    ln_phase(kb, "R1", "F2", mode="ln", eps=1e-5, dst_dtype=F32, tag="s2ln")
    g1 = Tile(kb, "sbF", 128, NCH, F32)
    load_colvec(kb, g1, "l0f_ln1", D, tag="s2g1")
    ln_phase(kb, "F2", "Y2", mode="rms", eps=1e-5, gamma_tile=g1, tag="s2rm")
    g1.free()
    if stop():
        return
    Ysrc = load_act_tiles(kb, "Y2", D, tag="s2Y")
    kvd = [Tile(kb, "sbB", 128, T, BF16) for _ in range(4)]
    proj_phase(kb, 4, [dict(w="l0f_kv_d", K=D, src=Ysrc, cols=128, row0=0)],
               dst_tiles=kvd, tag="s2kd")
    qd = [Tile(kb, "sbB", 128, T, BF16) for _ in range(4)]
    proj_phase(kb, 4, [dict(w="l0f_q_d", K=D, src=Ysrc, cols=128, row0=0)],
               dst_tiles=qd, tag="s2qd")
    if stop():
        return
    proj_phase(kb, 16, [dict(w="l0f_q_u", K=512, src=qd, cols=64, row0=0),
                        dict(w="l0f_rope_q", K=512, src=qd, cols=64, row0=64)],
               dst="Q2", rope=("l0f", False), tag="s2q",
               dbg_xr="DBGXR" if globals().get("DBG") else None)
    if stop():
        return
    proj_phase(kb, 16, [dict(w="l0f_k_u", K=512, src=kvd, cols=64, row0=0),
                        dict(w="l0f_rope_k", K=D, src=Ysrc, cols=64, row0=64)],
               dst="K2", rope=("l0f", False), tag="s2k")
    for t_ in Ysrc + qd:
        t_.free()
    if stop():
        return
    proj_phase(kb, 16, [dict(w="l0f_v_u", K=512, src=kvd, cols=128, row0=0)],
               dst="V2", tag="s2v")
    for t_ in kvd:
        t_.free()
    if stop():
        return
    kh = Tile(kb, "sbB", 128, T, BF16)
    vh = Tile(kb, "sbB", 128, T, BF16)
    vtmh = Tile(kb, "sbB", 128, T, BF16)
    identc = kb.const("ident", kb._hostc["ident"])

    def mkload(h):
        def load(kb2):
            load_rows(kb2, kh, "K2", h * 128, 128, tag="s2fk")
            load_rows(kb2, vh, "V2", h * 128, 128, tag="s2fv")
            for tb in range(0, 16, 8):
                def trf(e, tb=tb):
                    ins = None
                    for j in range(8):
                        ins = e.transpose(kb2.bankB(j * 128, 128),
                                          vh[:, (tb + j) * 128:(tb + j + 1) * 128],
                                          identc[0:128, 0:128])
                    return ins
                kb2.op("t", trf, [vh.r(), identc.r()], [kb2.bankB_r(0, 1024)],
                       tag="s2ft")
                kb2.op("v", lambda e, tb=tb: e.tensor_copy(
                    vtmh[:, tb * 128:(tb + 8) * 128], kb2.psB.ap()[:, 0:1024]),
                    [kb2.bankB_r(0, 1024)], [vtmh.r(tb * 128, (tb + 8) * 128)],
                    tag="s2fe")
        return load

    heads2 = [dict(c=h, base=0, load=mkload(h)) for h in range(16)]
    flash_phase(kb, 16, 128, "Q2", "AT2", 1.0 / np.sqrt(128.0), heads2,
                k_of=lambda h: (kh, 0), v_of=lambda h: (vtmh, 0), tag="s2f")
    kh.free(); vh.free(); vtmh.free()
    if stop():
        return
    AT2src = load_act_tiles(kb, "AT2", D, tag="s2at")
    proj_phase(kb, 16, [dict(w="l0f_o_w", K=D, src=AT2src, cols=128, row0=0)],
               dst="H2", base="F2", dst_dtype=F32, tag="s2o")
    for t_ in AT2src:
        t_.free()
    if stop():
        return
    g2 = Tile(kb, "sbF", 128, NCH, F32)
    load_colvec(kb, g2, "l0f_ln2", D, tag="s2g2")
    ln_phase(kb, "H2", "Z2", mode="rms", eps=1e-5, gamma_tile=g2, tag="s2r2")
    g2.free()
    if stop():
        return
    Zsrc = load_act_tiles(kb, "Z2", D, tag="s2Z")
    wT = router_phase(kb, Zsrc, "l0f_router_w", 7, bias_name="l0f_router_b",
                      act="sigmoid", tag="s2rt")
    if "WT" in kb.hbm:
        store_rows(kb, "WT", 0, 128, wT, tag="dbgwt")
    experts = [dict(gate=f"l0f_rg{e}", up=f"l0f_ru{e}", weighted=True,
                    hidoff=e * 1024) for e in range(7)]
    experts.append(dict(gate="l0f_shg", up="l0f_shu", weighted=False,
                        hidoff=7 * 1024))
    if stop():
        return
    moe_gateup(kb, Zsrc, experts, "HID", 0.125, wT=wT, tag="s2gu")
    for t_ in Zsrc:
        t_.free()
    wT.free()
    downs = [(f"l0f_rd{e}", e * 1024) for e in range(7)] + [("l0f_shd", 7 * 1024)]
    if stop():
        return
    moe_down(kb, downs, "HID", "G2", base="H2", tag="s2md")
    if stop():
        return
    ln_phase(kb, "G2", "R2", mode="ln", eps=1e-5, base="R1", src_is_bf=True,
             dst_dtype=F32, tag="s2r")



def linattn_phase(kb, kfr, vfr, qf_hbm, out_hbm, tag="la"):
    ident = kb.const("ident", kb._hostc["ident"])
    onesm_b = kb.const("ones_mat", kb._hostc["ones_mat"])
    maskm = kb.const("mask_mul", kb._hostc["mask_mul"])
    eps_t = kb.const("eps_la", np.full((128, 1), 1e-6, dtype=np.float32))

    qfc = Tile(kb, "sbB", 128, 16 * 128, BF16)
    kftm = [Tile(kb, "sbB", 128, 128, BF16) for _ in range(4)]
    vaug = [Tile(kb, "sbB", 128, 129, BF16) for _ in range(4)]
    Saug = [Tile(kb, "sbB", 128, 129, BF16) for _ in range(4)]
    zrep = Tile(kb, "sbB", 128, 128, BF16)
    zf32 = [Tile(kb, "sbF", 128, 1, F32) for _ in range(4)]
    Asb = Tile(kb, "sbB", 128, 128, BF16)
    dtmp = Tile(kb, "sbF", 128, 128, F32)
    rlt = Tile(kb, "sbF", 128, 128, F32)
    OTc = Tile(kb, "sbB", 128, 16 * 128, BF16)

    for g in range(4):
        kb.op("v", lambda e, g=g: e.memset(Saug[g].ap, 0.0),
              [], [Saug[g].r()], tag=tag + "z0")
        kb.op("v", lambda e, g=g: e.memset(zf32[g].ap, 0.0),
              [], [zf32[g].r()], tag=tag + "zz")
        kb.op("v", lambda e, g=g: e.memset(vaug[g][:, 128:129], 1.0),
              [], [vaug[g].r(128, 129)], tag=tag + "v1")

    for c in range(16):
        c0 = c * 128
        load_feat_cols(kb, qfc, qf_hbm, D, c0, c0 + 128, tag=tag + "q")
        def trf(e, c0=c0):
            ins = None
            for g in range(4):
                ins = e.transpose(kb.bankB(g * 128, 128),
                                  kfr[g][:, c0:c0 + 128], ident[0:128, 0:128])
            for g in range(4):
                ins = e.transpose(kb.bankB(512 + g * 128, 128),
                                  vfr[g][:, c0:c0 + 128], ident[0:128, 0:128])
            return ins
        kb.op("t", trf, [t.r() for t in kfr] + [t.r() for t in vfr] + [ident.r()],
              [kb.bankB_r(0, 1024)], tag=tag + "tr")
        for g in range(4):
            kb.op("v", lambda e, g=g: e.tensor_copy(kftm[g].ap,
                                                    kb.bankB(g * 128, 128)),
                  [kb.bankB_r(0, 1024)], [kftm[g].r()], tag=tag + "te")
            kb.op("v", lambda e, g=g: e.tensor_copy(vaug[g][:, 0:128],
                                                    kb.bankB(512 + g * 128, 128)),
                  [kb.bankB_r(0, 1024)], [vaug[g].r(0, 128)], tag=tag + "ve")
        for h in range(16):
            g = h // 4
            qsl = qfc[:, h * 128:(h + 1) * 128]
            qr = [qfc.r(h * 128, (h + 1) * 128)]
            kb.op("t", lambda e, g=g, c0=c0, qsl=qsl: e.matmul(
                kb.bankF(4)[:, 0:128], kfr[g][:, c0:c0 + 128], qsl,
                start=True, stop=True),
                [kfr[g].r(c0, c0 + 128)] + qr, [kb.bankF_r(4)], tag=tag + "a")
            kb.op("v", lambda e: e.tensor_mul(Asb.ap, kb.bankF(4)[:, 0:128],
                                              maskm.ap),
                  [kb.bankF_r(4), maskm.r()], [Asb.r()], tag=tag + "am")
            def nmm(e, g=g, qsl=qsl):
                e.matmul(kb.bankF(5)[:, 0:128], Saug[g][:, 0:128], qsl,
                         start=True, stop=False)
                return e.matmul(kb.bankF(5)[:, 0:128], vaug[g][:, 0:128],
                                Asb.ap, start=False, stop=True)
            kb.op("t", nmm, [Saug[g].r(0, 128), vaug[g].r(0, 128), Asb.r()] + qr,
                  [kb.bankF_r(5)], tag=tag + "n")
            kb.op("v", lambda e, g=g: e.tensor_scalar(
                zrep.ap, onesm_b[0:128, 0:128], zf32[g][0:128, 0:1], None,
                op0=ALU.mult),
                [onesm_b.r(), zf32[g].r()], [zrep.r()],
                tag=tag + "zr", pre_drain=True)
            def dmm(e, qsl=qsl):
                e.matmul(kb.bankF(6)[:, 0:128], zrep.ap, qsl,
                         start=True, stop=False)
                return e.matmul(kb.bankF(6)[:, 0:128], onesm_b[0:128, 0:128],
                                Asb.ap, start=False, stop=True)
            kb.op("t", dmm, [zrep.r(), onesm_b.r(), Asb.r()] + qr,
                  [kb.bankF_r(6)], tag=tag + "d")
            kb.op("v", lambda e: e.tensor_scalar(
                dtmp.ap, kb.bankF(6)[:, 0:128], eps_t[0:128, 0:1], None,
                op0=ALU.add),
                [kb.bankF_r(6), eps_t.r()], [dtmp.r()], tag=tag + "de")
            kb.op("v", lambda e: e.reciprocal(rlt.ap, dtmp.ap),
                  [dtmp.r()], [rlt.r()], tag=tag + "dr")
            kb.op("v", lambda e, h=h: e.tensor_mul(
                OTc[:, h * 128:(h + 1) * 128], kb.bankF(5)[:, 0:128], rlt.ap),
                [kb.bankF_r(5), rlt.r()], [OTc.r(h * 128, (h + 1) * 128)],
                tag=tag + "o", pre_drain=True)
        for g in range(4):
            kb.op("t", lambda e, g=g, c=c: e.matmul(
                kb.bankF(g)[:, 0:129], kftm[g].ap, vaug[g][:, 0:129],
                start=(c == 0), stop=(c == 15)),
                [kftm[g].r(), vaug[g].r()], [kb.bankF_r(g)], tag=tag + "su")
            kb.op("s", lambda e, g=g: e.activation(
                Saug[g].ap, kb.bankF(g)[:, 0:129], AF.Identity),
                [kb.bankF_r(g)], [Saug[g].r()], tag=tag + "se")
            kb.op("s", lambda e, g=g: e.activation(
                zf32[g].ap, kb.bankF(g)[:, 128:129], AF.Identity),
                [kb.bankF_r(g)], [zf32[g].r()], tag=tag + "ze")
        store_feat_cols(kb, out_hbm, c0, c0 + 128, OTc, D, tag=tag + "st")
    for t_ in [qfc, zrep, Asb, dtmp, rlt, OTc] + kftm + vaug + Saug + zf32:
        t_.free()


def declare_slot3(kb, out_kind="Internal", in_kind="Internal"):
    if "R2" not in kb.hbm:
        kb.hbm_tensor("R2", [D, T], F32, kind=in_kind)
    w = kb.hbm_tensor
    w("l1_WqT", [D, D], F32, kind="ExternalInput")
    w("l1_WkT", [D, 512], F32, kind="ExternalInput")
    w("l1_WvT", [D, 512], F32, kind="ExternalInput")
    w("l1_WoT", [D, D], F32, kind="ExternalInput")
    w("A3", [D, T], BF16, kind=out_kind)
    w("QF", [D, T], BF16, kind=out_kind)
    w("ATT3", [D, T], BF16, kind=out_kind)
    w("O3", [D, T], BF16, kind=out_kind)
    kb.hbm_tensor("R3", [D, T], F32, kind=out_kind)


def host_weights_slot3(inp):
    return {
        "l1_WqT": np.ascontiguousarray(inp["L1_attn_Wq"].T),
        "l1_WkT": np.ascontiguousarray(inp["L1_attn_Wk"].T),
        "l1_WvT": np.ascontiguousarray(inp["L1_attn_Wv"].T),
        "l1_WoT": np.ascontiguousarray(inp["L1_attn_Wo"].T),
    }


def emit_slot3(kb):
    ln_phase(kb, "R2", "A3", mode="ln", eps=1e-5, tag="s3ln")
    Asrc = load_act_tiles(kb, "A3", D, tag="s3A")
    proj_phase(kb, 16, [dict(w="l1_WqT", K=D, src=Asrc, cols=128, row0=0)],
               dst="QF", rope=("l1", True), tag="s3q")
    kfr = [Tile(kb, "sbB", 128, T, BF16) for _ in range(4)]
    proj_phase(kb, 4, [dict(w="l1_WkT", K=D, src=Asrc, cols=128, row0=0)],
               dst_tiles=kfr, rope=("l1", True), tag="s3k")
    vfr = [Tile(kb, "sbB", 128, T, BF16) for _ in range(4)]
    proj_phase(kb, 4, [dict(w="l1_WvT", K=D, src=Asrc, cols=128, row0=0)],
               dst_tiles=vfr, tag="s3v")
    for t_ in Asrc:
        t_.free()
    linattn_phase(kb, kfr, vfr, "QF", "ATT3", tag="s3la")
    for t_ in kfr + vfr:
        t_.free()
    ATsrc = load_act_tiles(kb, "ATT3", D, tag="s3at")
    proj_phase(kb, 16, [dict(w="l1_WoT", K=D, src=ATsrc, cols=128, row0=0)],
               dst="O3", tag="s3o")
    for t_ in ATsrc:
        t_.free()
    ln_phase(kb, "O3", "R3", mode="ln", eps=1e-5, base="R2", src_is_bf=True,
             dst_dtype=F32, tag="s3r")



def declare_slot4(kb, out_kind="Internal", in_kind="Internal"):
    if "R3" not in kb.hbm:
        kb.hbm_tensor("R3", [D, T], F32, kind=in_kind)
    w = kb.hbm_tensor
    w("l1f_rms", [128, NCH], F32, kind="ExternalInput")
    w("l1f_router_w", [D, 128], F32, kind="ExternalInput")
    for e in range(8):
        w(f"l1f_rg{e}", [D, 1024], F32, kind="ExternalInput")
        w(f"l1f_ru{e}", [D, 1024], F32, kind="ExternalInput")
        w(f"l1f_rd{e}", [1024, D], F32, kind="ExternalInput")
    w("l1f_swg", [D, 1024], F32, kind="ExternalInput")
    w("l1f_swu", [D, 1024], F32, kind="ExternalInput")
    w("l1f_swd", [1024, D], F32, kind="ExternalInput")
    w("M4", [D, T], BF16, kind=out_kind)
    w("HN4", [D, T], BF16, kind=out_kind)
    if "HID" not in kb.hbm:
        w("HID", [9 * 1024, T], BF16, kind="Internal")
    w("G4", [D, T], BF16, kind=out_kind)
    kb.hbm_tensor("OUTT", [D, T], F32, kind="ExternalOutput")


def host_weights_slot4(inp):
    return {
        "l1f_rms": np.ascontiguousarray(inp["L1_ffn_rms_weight"].reshape(NCH, 128).T),
        "l1f_router_w": np.ascontiguousarray(np.pad(inp["L1_ffn_router_w"],
                                                    ((0, 0), (0, 120)))),
        "l1f_swg": inp["L1_ffn_sw_gate"],
        "l1f_swu": inp["L1_ffn_sw_up"],
        "l1f_swd": inp["L1_ffn_sw_down"],
        **{f"l1f_rg{e}": inp["L1_ffn_w_gate"][e] for e in range(8)},
        **{f"l1f_ru{e}": inp["L1_ffn_w_up"][e] for e in range(8)},
        **{f"l1f_rd{e}": inp["L1_ffn_w_down"][e] for e in range(8)},
    }


def emit_slot4(kb):
    ln_phase(kb, "R3", "M4", mode="ln", eps=1e-5, tag="s4ln")
    g4 = Tile(kb, "sbF", 128, NCH, F32)
    load_colvec(kb, g4, "l1f_rms", D, tag="s4g")
    ln_phase(kb, "M4", "HN4", mode="rms", eps=1e-6, gamma_tile=g4,
             src_is_bf=True, tag="s4rm")
    g4.free()
    Hsrc = load_act_tiles(kb, "HN4", D, tag="s4H")
    wT = router_phase(kb, Hsrc, "l1f_router_w", 8, act="softmax", tag="s4rt")
    experts = [dict(gate=f"l1f_rg{e}", up=f"l1f_ru{e}", weighted=True,
                    hidoff=e * 1024) for e in range(8)]
    experts.append(dict(gate="l1f_swg", up="l1f_swu", weighted=False,
                        hidoff=8 * 1024))
    moe_gateup(kb, Hsrc, experts, "HID", None, wT=wT, nE=8, tag="s4gu")
    for t_ in Hsrc:
        t_.free()
    wT.free()
    downs = [(f"l1f_rd{e}", e * 1024) for e in range(8)] + [("l1f_swd", 8 * 1024)]
    moe_down(kb, downs, "HID", "G4", base=None, tag="s4md")
    ln_phase(kb, "G4", "OUTT", mode="ln", eps=1e-5, base="R3", src_is_bf=True,
             dst_dtype=F32, tag="s4r")



def build_full():
    kb = KB()
    declare_slot1(kb, out_kind="Internal")
    declare_slot2(kb, out_kind="Internal")
    declare_slot3(kb, out_kind="Internal")
    declare_slot4(kb, out_kind="Internal")
    emit_slot1(kb)
    emit_slot2(kb)
    emit_slot3(kb)
    emit_slot4(kb)
    finish(kb)
    return kb


def host_weights_all(inputs):
    wts = {}
    wts.update(host_weights_slot1(inputs))
    wts.update(host_weights_slot2(inputs))
    wts.update(host_weights_slot3(inputs))
    wts.update(host_weights_slot4(inputs))
    return wts


def kernel(**inputs):
    kb = build_full()
    wts = host_weights_all(inputs)
    x = np.asarray(inputs["L0_attn_hidden_states"])
    in_maps = []
    for b in range(8):
        im = {"XT": np.ascontiguousarray(x[b].T)}
        im.update(wts)
        im.update(kb.consts)
        in_maps.append(im)
    res = run_bass_kernel_spmd(kb.nc, in_maps, core_ids=list(range(8)))
    out = np.stack([np.asarray(res.results[b]["OUTT"]).T for b in range(8)])
    return np.ascontiguousarray(out.astype(np.float32))
